# version 09
# baseline (speedup 1.0000x reference)
from contextlib import ExitStack

import numpy as np
import ml_dtypes
import concourse.bass as bass
import concourse.mybir as mybir
from concourse.bass_utils import run_bass_kernel_spmd

F32 = mybir.dt.float32
BF16 = mybir.dt.bfloat16
AF = mybir.ActivationFunctionType
ALU = mybir.AluOpType
BF = ml_dtypes.bfloat16
AX = mybir.AxisListType

H = 2048
NK = 16
CH = 512
NCH = 8
TCORE = 4096


def build_nc():
    nc = bass.Bass("TRN2", target_bir_lowering=False)

    x_in = nc.dram_tensor("x", [TCORE, H], F32, kind="ExternalInput").ap()
    wxy_blk = nc.dram_tensor("wxy_blk", [32, 128, 2048], F32, kind="ExternalInput").ap()
    wr_blk = nc.dram_tensor("wr_blk", [16, 128, 2048], F32, kind="ExternalInput").ap()
    w13_all = nc.dram_tensor("w13_all", [9, 8, 2, 128, 2048], F32, kind="ExternalInput").ap()
    w2_all = nc.dram_tensor("w2_all", [9, 16, 128, 1024], F32, kind="ExternalInput").ap()
    gw_t = nc.dram_tensor("gw_t", [NK, 128, 8], F32, kind="ExternalInput").ap()
    igw_in = nc.dram_tensor("igw", [16, 128, 128], F32, kind="ExternalInput").ap()
    agw_in = nc.dram_tensor("agw", [16, 128, 128], F32, kind="ExternalInput").ap()
    igb_in = nc.dram_tensor("igb_t", [128, 16], F32, kind="ExternalInput").ap()
    agb_in = nc.dram_tensor("agb_t", [128, 16], F32, kind="ExternalInput").ap()
    cw_in = nc.dram_tensor("cw_t", [16, 128, 4], F32, kind="ExternalInput").ap()
    cb_in = nc.dram_tensor("cb_t", [128, 16], F32, kind="ExternalInput").ap()
    ap_in = nc.dram_tensor("ap_t", [128, 16], F32, kind="ExternalInput").ap()
    idf_in = nc.dram_tensor("ident_f32", [128, 128], F32, kind="ExternalInput").ap()
    idb_in = nc.dram_tensor("ident_bf", [128, 128], BF16, kind="ExternalInput").ap()
    oh_in = nc.dram_tensor("onehot", [8, 1024], BF16, kind="ExternalInput").ap()

    out_t = nc.dram_tensor("out", [TCORE, H], F32, kind="ExternalOutput").ap()
    wxy_bf = nc.dram_tensor("wxy_bf", [32, 128, 2048], BF16, kind="Internal").ap()
    wr_bf = nc.dram_tensor("wr_bf", [16, 128, 2048], BF16, kind="Internal").ap()
    w13_bf = nc.dram_tensor("w13_bf", [9, 8, 2, 128, 2048], BF16, kind="Internal").ap()
    w2_bf = nc.dram_tensor("w2_bf", [9, 16, 128, 1024], BF16, kind="Internal").ap()

    C = {"SD": 0, "GD": 0, "PT": 0, "PV": 0, "PS": 0, "WD": 0}
    bank_guard = {}
    buf_guard = {}

    with ExitStack() as _st:
        E = _st.enter_context
        ident_f = E(nc.sbuf_tensor("ident_f", [128, 128], F32))
        ident_b = E(nc.sbuf_tensor("ident_b", [128, 128], BF16))
        onehot_sb = E(nc.sbuf_tensor("onehot_sb", [8, 1024], BF16))
        igw_sb = E(nc.sbuf_tensor("igw_sb", [128, 2048], BF16))
        agw_sb = E(nc.sbuf_tensor("agw_sb", [128, 2048], BF16))
        igb_sb = E(nc.sbuf_tensor("igb_sb", [128, 16], F32))
        agb_sb = E(nc.sbuf_tensor("agb_sb", [128, 16], F32))
        cw_sb = E(nc.sbuf_tensor("cw_sb", [128, 64], F32))
        cb_sb = E(nc.sbuf_tensor("cb_sb", [128, 16], F32))
        ap_sb = E(nc.sbuf_tensor("ap_sb", [128, 16], F32))
        c8_sb = E(nc.sbuf_tensor("c8_sb", [128, 16], F32))
        sc16a = E(nc.sbuf_tensor("sc16a", [128, 16], F32))
        sc16b = E(nc.sbuf_tensor("sc16b", [128, 16], F32))
        eps_col = E(nc.sbuf_tensor("eps_col", [128, 1], F32))
        gwT_sb = E(nc.sbuf_tensor("gwT_sb", [128, 128], BF16))

        xnT = E(nc.sbuf_tensor("xnT", [128, NK * CH], BF16))
        gyv = E(nc.sbuf_tensor("gyv", [128, NK * CH], BF16))
        acc = E(nc.sbuf_tensor("acc", [128, NK * CH], F32))
        xc16 = E(nc.sbuf_tensor("xc16", [128, NK * CH], F32))
        xcb_rot = E(nc.sbuf_tensor("xcb_rot", [128, 4 * CH], BF16))
        g_sb = E(nc.sbuf_tensor("g_sb", [128, 8 * CH], BF16))
        comb_sb = E(nc.sbuf_tensor("comb_sb", [128, 8 * CH], BF16))
        ga_g = E(nc.sbuf_tensor("ga_g", [128, 4 * CH], F32))
        u_g = E(nc.sbuf_tensor("u_g", [128, 4 * CH], F32))
        gx_g = E(nc.sbuf_tensor("gx_g", [128, 4 * CH], BF16))
        conv_carry = E(nc.sbuf_tensor("conv_carry", [128, 48], F32))
        scan_carry = E(nc.sbuf_tensor("scan_carry", [128, 16], F32))
        x_tm0 = E(nc.sbuf_tensor("x_tm0", [128, H], F32))
        x_tm1 = E(nc.sbuf_tensor("x_tm1", [128, H], F32))
        xm_t = E(nc.sbuf_tensor("xm_t", [128, H], F32))
        xnbf = E(nc.sbuf_tensor("xnbf", [128, H], BF16))
        mcol = E(nc.sbuf_tensor("mcol", [128, 1], F32))
        vcol = E(nc.sbuf_tensor("vcol", [128, 1], F32))
        wb0 = E(nc.sbuf_tensor("wb0", [128, 2048], BF16))
        wb1 = E(nc.sbuf_tensor("wb1", [128, 2048], BF16))
        wb2 = E(nc.sbuf_tensor("wb2", [128, 2048], BF16))
        wb3 = E(nc.sbuf_tensor("wb3", [128, 2048], BF16))
        v2b0 = E(nc.sbuf_tensor("v2b0", [128, 1024], BF16))
        v2b1 = E(nc.sbuf_tensor("v2b1", [128, 1024], BF16))
        v2b2 = E(nc.sbuf_tensor("v2b2", [128, 1024], BF16))
        v2b3 = E(nc.sbuf_tensor("v2b3", [128, 1024], BF16))
        silu_b = E(nc.sbuf_tensor("silu_b", [128, CH], F32))
        scores = E(nc.sbuf_tensor("scores", [128, 32], F32))
        escr = E(nc.sbuf_tensor("escr", [128, 32], F32))
        p_sb = E(nc.sbuf_tensor("p_sb", [128, 32], F32))
        pm_sb = E(nc.sbuf_tensor("pm_sb", [128, 32], F32))
        combtm = E(nc.sbuf_tensor("combtm", [128, 32], F32))
        stat4a = E(nc.sbuf_tensor("stat4a", [128, 4], F32))
        stat4b = E(nc.sbuf_tensor("stat4b", [128, 4], F32))
        stat4c = E(nc.sbuf_tensor("stat4c", [128, 4], F32))
        scr8 = E(nc.sbuf_tensor("scr8", [128, 8], F32))
        comb_fm = E(nc.sbuf_tensor("comb_fm", [8, 512], BF16))
        cast_buf = E(nc.sbuf_tensor("cast_buf", [128, 2048], BF16))
        psum = E(nc.psum_tensor("psum", [128, 4096], F32))
        sd = E(nc.semaphore("sd"))
        wd = E(nc.semaphore("wd"))
        gd = E(nc.semaphore("gd"))
        pt = E(nc.semaphore("pt"))
        pv = E(nc.semaphore("pv"))
        ps = E(nc.semaphore("ps"))
        block = E(nc.Block())
        psum_bf = psum[:].bitcast(BF16)

        wbs = [wb0, wb1, wb2, wb3]
        v2bs = [v2b0, v2b1, v2b2, v2b3]
        x_tms = [x_tm0, x_tm1]

        def bank(b):
            return psum[:, b * 512:(b + 1) * 512]

        pair_bf = psum_bf[:, 3 * 2048:4 * 2048]

        SEMS = {"pv": pv, "ps": ps, "sd": sd, "pt": pt, "gd": gd, "wd": wd}

        def buf_wait(eng, g):
            if g is not None:
                eng.wait_ge(SEMS[g[0]], g[1])

        def pe_wait(t, banks):
            need = {}
            for b in banks:
                if b in bank_guard:
                    s, v = bank_guard[b]
                    need[s] = max(need.get(s, 0), v)
            for s, v in need.items():
                t.wait_ge(SEMS[s], v)

        def _(sy):
            sy.dma_start(out=ident_f[:], in_=idf_in).then_inc(sd, 16)
            sy.dma_start(out=ident_b[:], in_=idb_in).then_inc(sd, 16)
            sy.dma_start(out=onehot_sb[:], in_=oh_in).then_inc(sd, 16)
            sy.dma_start(out=igb_sb[:], in_=igb_in).then_inc(sd, 16)
            sy.dma_start(out=agb_sb[:], in_=agb_in).then_inc(sd, 16)
            sy.dma_start(out=cw_sb[:].rearrange("p (n i) -> p n i", n=16), in_=cw_in.rearrange("n p i -> p n i")).then_inc(sd, 16)
            sy.dma_start(out=cb_sb[:], in_=cb_in).then_inc(sd, 16)
            sy.dma_start(out=ap_sb[:], in_=ap_in).then_inc(sd, 16)
        block.sync(_); C["SD"] += 16 * 8
        sd_prologue = C["SD"]

        def _(g):
            g.dma_start(out=igw_sb[:].rearrange("p (n j) -> p n j", n=16), in_=igw_in.rearrange("n i j -> i n j")).then_inc(gd, 16)
            g.dma_start(out=agw_sb[:].rearrange("p (n j) -> p n j", n=16), in_=agw_in.rearrange("n i j -> i n j")).then_inc(gd, 16)
            g.dma_start(out=gwT_sb[:].rearrange("p (k e) -> p k e", k=16), in_=gw_t.rearrange("k p e -> p k e")).then_inc(gd, 16)
        block.gpsimd(_); C["GD"] += 16 * 3
        gd_prologue = C["GD"]

        def _(s):
            s.wait_ge(sd, sd_prologue)
            s.activation(sc16a[:], ap_sb[:], AF.Abs)
            s.activation(sc16a[:], sc16a[:], AF.Exp, scale=-1.0)
            s.activation(sc16a[:], sc16a[:], AF.Ln, bias=1.0).then_inc(ps, 1)
        block.scalar(_); C["PS"] += 1
        ps_c8 = C["PS"]
        def _(v):
            v.wait_ge(ps, ps_c8)
            v.memset(eps_col[:], 1e-5)
            v.tensor_scalar_max(sc16b[:], ap_sb[:], 0.0).then_inc(pv, 1)
            v.wait_ge(pv, 1)
            v.tensor_add(c8_sb[:], sc16a[:], sc16b[:]).then_inc(pv, 1)
            v.wait_ge(pv, 2)
            v.tensor_scalar_mul(c8_sb[:], c8_sb[:], -8.0).then_inc(pv, 1)
        block.vector(_); C["PV"] += 3

        big_idx = [0]
        w2_idx = [0]
        chunk0 = [True]

        def _emit_tile(idx_box, bufs, kind, src_f32, src_bf):
            sl = idx_box[0] % 4
            buf = bufs[sl]
            gk = buf_guard.get((kind, sl))
            if chunk0[0]:
                def _(g, buf=buf, src=src_f32, gk=gk):
                    buf_wait(g, gk)
                    g.dma_start(out=buf[:], in_=src).then_inc(gd, 16)
                block.gpsimd(_); C["GD"] += 16
                waits = (("gd", C["GD"]),)
                pend = src_bf
            else:
                if idx_box[0] % 2 == 0:
                    def _(sy, buf=buf, src=src_bf, gk=gk):
                        buf_wait(sy, gk)
                        sy.dma_start(out=buf[:], in_=src).then_inc(wd, 16)
                    block.sync(_); C["WD"] += 16
                    waits = (("wd", C["WD"]),)
                else:
                    def _(g, buf=buf, src=src_bf, gk=gk):
                        buf_wait(g, gk)
                        g.dma_start(out=buf[:], in_=src).then_inc(gd, 16)
                    block.gpsimd(_); C["GD"] += 16
                    waits = (("gd", C["GD"]),)
                pend = None
            idx_box[0] += 1
            return waits, buf, sl, pend

        def emit_big_dma(src_f32, src_bf):
            return _emit_tile(big_idx, wbs, "wb", src_f32, src_bf)

        def emit_w2_dma(src_f32, src_bf):
            return _emit_tile(w2_idx, v2bs, "v2b", src_f32, src_bf)

        def finish_tile(kind, sl, buf, pend, w_pt):
            if pend is not None:
                def _(sy, buf=buf, dst=pend, w=w_pt):
                    sy.wait_ge(pt, w)
                    sy.dma_start(out=dst, in_=buf[:]).then_inc(wd, 16)
                block.sync(_); C["WD"] += 16
                buf_guard[(kind, sl)] = ("wd", C["WD"])
            else:
                buf_guard[(kind, sl)] = ("pt", w_pt)

        def tile_waits(t, waits):
            for s, v in waits:
                t.wait_ge(SEMS[s], v)

        for c in range(NCH):
            row0 = c * CH
            batch_start = (c % 4 == 0)

            for tt in range(4):
                x_tm = x_tms[tt % 2]
                def _(sy, x_tm=x_tm, tt=tt, row0=row0, g=buf_guard.get(("x_tm", tt % 2))):
                    buf_wait(sy, g)
                    sy.dma_start(out=x_tm[:], in_=x_in[row0 + tt * 128: row0 + (tt + 1) * 128, :]).then_inc(sd, 16)
                block.sync(_); C["SD"] += 16
                sd_x = C["SD"]

                def _(v, x_tm=x_tm, sd_x=sd_x, c0=C["PV"]):
                    v.wait_ge(sd, sd_x)
                    v.reduce_sum(mcol[:], x_tm[:], AX.X).then_inc(pv, 1)
                    v.wait_ge(pv, c0 + 1)
                    v.tensor_scalar_mul(mcol[:], mcol[:], 1.0 / H).then_inc(pv, 1)
                    v.wait_ge(pv, c0 + 2)
                    v.tensor_scalar(xm_t[:], x_tm[:], mcol[:], None, ALU.subtract).then_inc(pv, 1)
                block.vector(_); C["PV"] += 3
                pv_xm = C["PV"]

                def _(s, x_tm=x_tm, pv_xm=pv_xm):
                    s.wait_ge(pv, pv_xm)
                    s.activation(x_tm[:], xm_t[:], AF.Square, accum_out=vcol[:])
                    s.activation(vcol[:], vcol[:], AF.Ln, scale=1.0 / H, bias=eps_col[:])
                    s.activation(vcol[:], vcol[:], AF.Exp, scale=-0.5).then_inc(ps, 1)
                block.scalar(_); C["PS"] += 1
                ps_rstd = C["PS"]
                buf_guard[("x_tm", tt % 2)] = ("ps", ps_rstd)

                def _(v, ps_rstd=ps_rstd, gnb=buf_guard.get("xnbf")):
                    v.wait_ge(ps, ps_rstd)
                    buf_wait(v, gnb)
                    v.tensor_scalar(xnbf[:], xm_t[:], vcol[:], None, ALU.mult).then_inc(pv, 1)
                block.vector(_); C["PV"] += 1
                pv_xn = C["PV"]

                def _(t, pv_xn=pv_xn, tt=tt, gx=buf_guard.get("xnT") if tt == 0 else None):
                    t.wait_ge(pv, pv_xn)
                    if gx is not None:
                        buf_wait(t, gx)
                    pe_wait(t, [6, 7])
                    for k in range(NK):
                        ins = t.transpose(pair_bf[:, k * 128:(k + 1) * 128], xnbf[:, k * 128:(k + 1) * 128], ident_b[:])
                    ins.then_inc(pt, 1)
                block.tensor(_); C["PT"] += 1
                pt_tr = C["PT"]

                def _(v, tt=tt, w=pt_tr):
                    v.wait_ge(pt, w)
                    v.tensor_copy(
                        xnT[:].rearrange("p (k t) -> p k t", k=NK)[:, :, tt * 128:(tt + 1) * 128],
                        pair_bf.rearrange("p (k q) -> p k q", k=NK),
                    ).then_inc(pv, 1)
                block.vector(_); C["PV"] += 1
                bank_guard[6] = ("pv", C["PV"])
                bank_guard[7] = ("pv", C["PV"])
                buf_guard["xnbf"] = ("pt", pt_tr)
            pv_xnT = C["PV"]

            for tloc in range(4):
                b = 4 + tloc % 2
                def _(t, tloc=tloc, b=b, w=pv_xnT, first=(c == 0)):
                    if tloc == 0:
                        t.wait_ge(pv, w)
                        if first:
                            t.wait_ge(gd, gd_prologue)
                    pe_wait(t, [b])
                    for k in range(NK):
                        ins = t.matmul(psum[:, b * 512: b * 512 + 8],
                                       xnT[:, k * CH + tloc * 128: k * CH + (tloc + 1) * 128],
                                       gwT_sb[:, k * 8:(k + 1) * 8],
                                       start=(k == 0), stop=(k == NK - 1))
                    ins.then_inc(pt, 1)
                block.tensor(_); C["PT"] += 1
                def _(v, tloc=tloc, b=b, w=C["PT"]):
                    v.wait_ge(pt, w)
                    v.tensor_copy(scores[:, tloc * 8:(tloc + 1) * 8], psum[:, b * 512: b * 512 + 8]).then_inc(pv, 1)
                block.vector(_); C["PV"] += 1
                bank_guard[b] = ("pv", C["PV"])

            def _(v, w=C["PV"]):
                v.wait_ge(pv, w)
                v.reduce_max(stat4a[:].rearrange("p (t e) -> p t e", e=1),
                             scores[:].rearrange("p (t e) -> p t e", t=4),
                             AX.X, negate=True).then_inc(pv, 1)
            block.vector(_); C["PV"] += 1
            w_negmax = C["PV"]
            def _(s, w=w_negmax):
                s.wait_ge(pv, w)
                for tloc in range(4):
                    ins = s.activation(escr[:, tloc * 8:(tloc + 1) * 8], scores[:, tloc * 8:(tloc + 1) * 8], AF.Exp, bias=stat4a[:, tloc:tloc + 1])
                ins.then_inc(ps, 1)
            block.scalar(_); C["PS"] += 1
            w_exp = C["PS"]
            def _(v, w=w_exp, c0=C["PV"]):
                n = [c0]
                def step(ins):
                    n[0] += 1
                    ins.then_inc(pv, 1)
                    v.wait_ge(pv, n[0])
                v.wait_ge(ps, w)
                step(v.reduce_sum(stat4b[:].rearrange("p (t e) -> p t e", e=1),
                                  escr[:].rearrange("p (t e) -> p t e", t=4), AX.X))
                step(v.reciprocal(stat4c[:], stat4b[:]))
                for tloc in range(4):
                    sl = slice(tloc * 8, (tloc + 1) * 8)
                    ins = v.tensor_scalar(p_sb[:, sl], escr[:, sl], stat4c[:, tloc:tloc + 1], None, ALU.mult)
                step(ins)
                step(v.reduce_max(stat4a[:].rearrange("p (t e) -> p t e", e=1),
                                  p_sb[:].rearrange("p (t e) -> p t e", t=4), AX.X))
                for tloc in range(4):
                    sl = slice(tloc * 8, (tloc + 1) * 8)
                    step(v.tensor_scalar(scr8[:], p_sb[:, sl], stat4a[:, tloc:tloc + 1], None, ALU.is_equal))
                    step(v.scalar_tensor_tensor(pm_sb[:, sl], scr8[:], -1e30, p_sb[:, sl], ALU.mult, ALU.add))
                step(v.reduce_max(stat4b[:].rearrange("p (t e) -> p t e", e=1),
                                  pm_sb[:].rearrange("p (t e) -> p t e", t=4), AX.X))
                for tloc in range(4):
                    sl = slice(tloc * 8, (tloc + 1) * 8)
                    step(v.tensor_scalar(scr8[:], p_sb[:, sl], stat4b[:, tloc:tloc + 1], None, ALU.is_ge))
                    step(v.tensor_mul(combtm[:, sl], scr8[:], p_sb[:, sl]))
            block.vector(_); C["PV"] += 3 + 1 + 8 + 1 + 8
            w_comb = C["PV"]

            def emit_xy_mm(waits, buf, b, rhs):
                def _(t, b=b, buf=buf, waits=waits):
                    tile_waits(t, waits)
                    pe_wait(t, [b])
                    for k in range(NK):
                        ins = t.matmul(bank(b), buf[:, k * 128:(k + 1) * 128], rhs[:, k * CH:(k + 1) * CH], start=(k == 0), stop=(k == NK - 1))
                    ins.then_inc(pt, 1)
                block.tensor(_); C["PT"] += 1
                return C["PT"]

            for idx in range(16):
                r = 16 + idx
                waits, buf, sl, pend = emit_big_dma(wxy_blk[r], wxy_bf[r])
                b = idx % 2
                w_pt = emit_xy_mm(waits, buf, b, xnT)
                finish_tile("wb", sl, buf, pend, w_pt)
                def _(s, r=r, b=b, w=w_pt, gg=buf_guard.get("gyv") if idx == 0 else None):
                    s.wait_ge(pt, w)
                    if gg is not None:
                        buf_wait(s, gg)
                    s.activation(gyv[:, (r - 16) * 512:(r - 16 + 1) * 512], bank(b), AF.Gelu).then_inc(ps, 1)
                block.scalar(_); C["PS"] += 1
                bank_guard[b] = ("ps", C["PS"])

            for tloc in range(4):
                b = 4 + tloc % 2
                def _(t, tloc=tloc, b=b, w=w_comb):
                    if tloc == 0:
                        t.wait_ge(pv, w)
                    pe_wait(t, [b])
                    t.transpose(psum[0:8, b * 512: b * 512 + 128], combtm[:, tloc * 8:(tloc + 1) * 8], ident_f[:]).then_inc(pt, 1)
                block.tensor(_); C["PT"] += 1
                def _(v, tloc=tloc, b=b, w=C["PT"]):
                    v.wait_ge(pt, w)
                    v.tensor_copy(comb_fm[:, tloc * 128:(tloc + 1) * 128], psum[0:8, b * 512: b * 512 + 128]).then_inc(pv, 1)
                block.vector(_); C["PV"] += 1
                bank_guard[b] = ("pv", C["PV"])
            w_combfm = C["PV"]

            for e in range(8):
                b = 4 + e % 2
                def _(t, e=e, b=b, w=w_combfm):
                    if e == 0:
                        t.wait_ge(pv, w)
                    pe_wait(t, [b])
                    t.matmul(bank(b), onehot_sb[:, e * 128:(e + 1) * 128], comb_fm[:], start=True, stop=True).then_inc(pt, 1)
                block.tensor(_); C["PT"] += 1
                def _(v, e=e, b=b, w=C["PT"], gc=buf_guard.get("comb") if e == 0 else None):
                    v.wait_ge(pt, w)
                    if gc is not None:
                        buf_wait(v, gc)
                    v.tensor_copy(comb_sb[:, e * CH:(e + 1) * CH], bank(b)).then_inc(pv, 1)
                block.vector(_); C["PV"] += 1
                bank_guard[b] = ("pv", C["PV"])

            if batch_start:
                def _(v):
                    v.memset(conv_carry[:], 0.0)
                    v.memset(scan_carry[:], 0.0)
                block.vector(_)

            conv_done = [0] * 16
            for r in range(16):
                waits, buf, sl, pend = emit_big_dma(wxy_blk[r], wxy_bf[r])
                b = r % 2
                w_pt = emit_xy_mm(waits, buf, b, xnT)
                finish_tile("wb", sl, buf, pend, w_pt)
                def _(v, r=r, b=b, w=w_pt, gx=buf_guard.get("xc16") if r == 0 else None):
                    v.wait_ge(pt, w)
                    if gx is not None:
                        buf_wait(v, gx)
                    sl2 = slice(r * 512, (r + 1) * 512)
                    pb = bank(b)
                    v.tensor_scalar(xc16[:, sl2], pb, cw_sb[:, r * 4 + 3:r * 4 + 4], cb_sb[:, r:r + 1], ALU.mult, ALU.add)
                    v.scalar_tensor_tensor(xc16[:, r * 512 + 1:(r + 1) * 512], pb[:, 0:511], cw_sb[:, r * 4 + 2:r * 4 + 3], xc16[:, r * 512 + 1:(r + 1) * 512], ALU.mult, ALU.add)
                    v.scalar_tensor_tensor(xc16[:, r * 512 + 2:(r + 1) * 512], pb[:, 0:510], cw_sb[:, r * 4 + 1:r * 4 + 2], xc16[:, r * 512 + 2:(r + 1) * 512], ALU.mult, ALU.add)
                    v.scalar_tensor_tensor(xc16[:, r * 512 + 3:(r + 1) * 512], pb[:, 0:509], cw_sb[:, r * 4 + 0:r * 4 + 1], xc16[:, r * 512 + 3:(r + 1) * 512], ALU.mult, ALU.add)
                    cc = conv_carry[:, r * 3:(r + 1) * 3]
                    v.scalar_tensor_tensor(xc16[:, r * 512:r * 512 + 1], cc[:, 2:3], cw_sb[:, r * 4 + 2:r * 4 + 3], xc16[:, r * 512:r * 512 + 1], ALU.mult, ALU.add)
                    v.scalar_tensor_tensor(xc16[:, r * 512:r * 512 + 2], cc[:, 1:3], cw_sb[:, r * 4 + 1:r * 4 + 2], xc16[:, r * 512:r * 512 + 2], ALU.mult, ALU.add)
                    v.scalar_tensor_tensor(xc16[:, r * 512:r * 512 + 3], cc[:, 0:3], cw_sb[:, r * 4 + 0:r * 4 + 1], xc16[:, r * 512:r * 512 + 3], ALU.mult, ALU.add)
                    v.tensor_copy(cc, pb[:, 509:512]).then_inc(pv, 1)
                block.vector(_); C["PV"] += 1
                conv_done[r] = C["PV"]
                bank_guard[b] = ("pv", C["PV"])

            sig_done = [0] * 16
            la_done = [0] * 16
            asq_done = [0] * 16
            om_done = [0] * 16
            mult_done = [0] * 16
            scan_done = [0] * 16
            g_done = [0] * 9

            def emit_w13(e):
                done = 0
                for m in range(8):
                    wt1, buf1, sl1, pend1 = emit_big_dma(w13_all[e, m, 0], w13_bf[e, m, 0])
                    wt3, buf3, sl3, pend3 = emit_big_dma(w13_all[e, m, 1], w13_bf[e, m, 1])
                    def _(t, buf1=buf1, buf3=buf3, wt1=wt1, wt3=wt3):
                        tile_waits(t, wt1)
                        tile_waits(t, wt3)
                        pe_wait(t, [2, 3])
                        for k in range(NK):
                            ins = t.matmul(bank(2), buf1[:, k * 128:(k + 1) * 128], xnT[:, k * CH:(k + 1) * CH], start=(k == 0), stop=(k == NK - 1))
                        ins.then_inc(pt, 1)
                        for k in range(NK):
                            ins = t.matmul(bank(3), buf3[:, k * 128:(k + 1) * 128], xnT[:, k * CH:(k + 1) * CH], start=(k == 0), stop=(k == NK - 1))
                        ins.then_inc(pt, 1)
                    block.tensor(_); C["PT"] += 2
                    finish_tile("wb", sl1, buf1, pend1, C["PT"])
                    finish_tile("wb", sl3, buf3, pend3, C["PT"])
                    def _(s, w=C["PT"], gsl=buf_guard.get("silu_b")):
                        s.wait_ge(pt, w)
                        buf_wait(s, gsl)
                        s.activation(silu_b[:], bank(2), AF.Silu).then_inc(ps, 1)
                    block.scalar(_); C["PS"] += 1
                    bank_guard[2] = ("ps", C["PS"])
                    def _(v, e=e, m=m, w=C["PS"]):
                        v.wait_ge(ps, w)
                        if e < 8:
                            v.tensor_mul(silu_b[:], silu_b[:], bank(3))
                            v.tensor_mul(g_sb[:, m * CH:(m + 1) * CH], silu_b[:], comb_sb[:, e * CH:(e + 1) * CH]).then_inc(pv, 1)
                        else:
                            v.tensor_mul(g_sb[:, m * CH:(m + 1) * CH], silu_b[:], bank(3)).then_inc(pv, 1)
                    block.vector(_); C["PV"] += 1
                    bank_guard[3] = ("pv", C["PV"])
                    buf_guard["silu_b"] = ("pv", C["PV"])
                    done = C["PV"]
                return done

            def emit_w2(e, first):
                last = 0
                for m2 in range(16):
                    wt2, buf2, sl2, pend2 = emit_w2_dma(w2_all[e, m2], w2_bf[e, m2])
                    b = 4 + m2 % 2
                    def _(t, buf2=buf2, wt2=wt2, b=b, m2=m2, wg=g_done[e]):
                        tile_waits(t, wt2)
                        if m2 == 0:
                            t.wait_ge(pv, wg)
                        pe_wait(t, [b])
                        for k in range(8):
                            ins = t.matmul(bank(b), buf2[:, k * 128:(k + 1) * 128], g_sb[:, k * CH:(k + 1) * CH], start=(k == 0), stop=(k == 7))
                        ins.then_inc(pt, 1)
                    block.tensor(_); C["PT"] += 1
                    finish_tile("v2b", sl2, buf2, pend2, C["PT"])
                    def _(v, m2=m2, b=b, w=C["PT"], first=first, ga=buf_guard.get("acc") if (first and m2 == 0) else None):
                        v.wait_ge(pt, w)
                        if ga is not None:
                            buf_wait(v, ga)
                        sl = slice(m2 * CH, (m2 + 1) * CH)
                        if first:
                            v.tensor_copy(acc[:, sl], bank(b)).then_inc(pv, 1)
                        else:
                            v.tensor_add(acc[:, sl], acc[:, sl], bank(b)).then_inc(pv, 1)
                    block.vector(_); C["PV"] += 1
                    bank_guard[b] = ("pv", C["PV"])
                    last = C["PV"]
                return last

            def emit_wave(g4):
                cast_done = [0] * 4
                for j4 in range(4):
                    r = 4 * g4 + j4
                    def _(s, r=r, w=conv_done[r], gk=buf_guard.get(("xcb", r % 4))):
                        s.wait_ge(pv, w)
                        buf_wait(s, gk)
                        s.copy(xcb_rot[:, (r % 4) * 512:(r % 4 + 1) * 512], xc16[:, r * 512:(r + 1) * 512]).then_inc(ps, 1)
                    block.scalar(_); C["PS"] += 1
                    cast_done[j4] = C["PS"]
                for j4 in range(4):
                    r = 4 * g4 + j4
                    def _(t, r=r, w=cast_done[j4]):
                        t.wait_ge(ps, w)
                        pe_wait(t, [6, 7])
                        xs = xcb_rot[:, (r % 4) * 512:(r % 4 + 1) * 512]
                        t.matmul(bank(6), igw_sb[:, r * 128:(r + 1) * 128], xs, start=True, stop=True).then_inc(pt, 1)
                        t.matmul(bank(7), agw_sb[:, r * 128:(r + 1) * 128], xs, start=True, stop=True).then_inc(pt, 1)
                    block.tensor(_); C["PT"] += 2
                    buf_guard[("xcb", r % 4)] = ("pt", C["PT"])
                    def _(s, r=r, w=C["PT"], gsl=buf_guard.get(("gslot", r % 4))):
                        s.wait_ge(pt, w)
                        buf_wait(s, gsl)
                        j = r % 4
                        s.activation(gx_g[:, j * 512:(j + 1) * 512], bank(6), AF.Sigmoid, bias=igb_sb[:, r:r + 1])
                        s.activation(ga_g[:, j * 512:(j + 1) * 512], bank(7), AF.Sigmoid, bias=agb_sb[:, r:r + 1]).then_inc(ps, 1)
                    block.scalar(_); C["PS"] += 1
                    sig_done[r] = C["PS"]
                    bank_guard[6] = ("ps", C["PS"])
                    bank_guard[7] = ("ps", C["PS"])
                for j4 in range(4):
                    r = 4 * g4 + j4
                    def _(v, r=r, j=r % 4, w=sig_done[r]):
                        v.wait_ge(ps, w)
                        sl = slice(j * 512, (j + 1) * 512)
                        v.tensor_scalar(ga_g[:, sl], ga_g[:, sl], c8_sb[:, r:r + 1], None, ALU.mult).then_inc(pv, 1)
                    block.vector(_); C["PV"] += 1
                    la_done[r] = C["PV"]
                for j4 in range(4):
                    r = 4 * g4 + j4
                    def _(s, r=r, j=r % 4, w=la_done[r]):
                        s.wait_ge(pv, w)
                        sl = slice(j * 512, (j + 1) * 512)
                        s.activation(ga_g[:, sl], ga_g[:, sl], AF.Exp)
                        s.activation(u_g[:, sl], ga_g[:, sl], AF.Square).then_inc(ps, 1)
                    block.scalar(_); C["PS"] += 1
                    asq_done[r] = C["PS"]
                for j4 in range(4):
                    r = 4 * g4 + j4
                    def _(v, r=r, j=r % 4, w=asq_done[r]):
                        v.wait_ge(ps, w)
                        sl = slice(j * 512, (j + 1) * 512)
                        v.tensor_scalar(u_g[:, sl], u_g[:, sl], -1.0, 1.0, ALU.mult, ALU.add).then_inc(pv, 1)
                    block.vector(_); C["PV"] += 1
                    om_done[r] = C["PV"]
                for j4 in range(4):
                    r = 4 * g4 + j4
                    def _(s, r=r, j=r % 4, w=om_done[r]):
                        s.wait_ge(pv, w)
                        sl = slice(j * 512, (j + 1) * 512)
                        s.activation(u_g[:, sl], u_g[:, sl], AF.Ln)
                        s.activation(u_g[:, sl], u_g[:, sl], AF.Exp, scale=0.5).then_inc(ps, 1)
                    block.scalar(_); C["PS"] += 1
                    mult_done[r] = C["PS"]
                for j4 in range(4):
                    r = 4 * g4 + j4
                    def _(v, r=r, j=r % 4, w=mult_done[r], c0=C["PV"]):
                        v.wait_ge(ps, w)
                        sl = slice(j * 512, (j + 1) * 512)
                        xsl = slice(r * 512, (r + 1) * 512)
                        v.tensor_mul(xc16[:, xsl], xc16[:, xsl], gx_g[:, sl])
                        v.tensor_mul(xc16[:, xsl], xc16[:, xsl], u_g[:, sl])
                        v.tensor_tensor_scan(u_g[:, sl], ga_g[:, sl], xc16[:, xsl], scan_carry[:, r:r + 1], ALU.mult, ALU.add).then_inc(pv, 1)
                        v.wait_ge(pv, c0 + 1)
                        v.tensor_copy(scan_carry[:, r:r + 1], u_g[:, j * 512 + 511:j * 512 + 512])
                        v.tensor_mul(gyv[:, xsl], gyv[:, xsl], u_g[:, sl]).then_inc(pv, 1)
                    block.vector(_); C["PV"] += 2
                    scan_done[r] = C["PV"]
                    buf_guard[("gslot", r % 4)] = ("pv", C["PV"])

            g_done[0] = emit_w13(0)
            emit_wave(0)
            emit_w2(0, first=True)
            for e in range(1, 4):
                g_done[e] = emit_w13(e)
                emit_wave(e)
                emit_w2(e, first=False)
            for e in range(4, 9):
                g_done[e] = emit_w13(e)
                emit_w2(e, first=False)
            tail_scan = scan_done[15]

            for m in range(16):
                waits, buf, slw, pendw = emit_big_dma(wr_blk[m], wr_bf[m])
                b = m % 2
                def _(t, b=b, buf=buf, waits=waits, m=m, ws=tail_scan):
                    tile_waits(t, waits)
                    if m == 0:
                        t.wait_ge(pv, ws)
                    pe_wait(t, [b])
                    for k in range(NK):
                        ins = t.matmul(bank(b), buf[:, k * 128:(k + 1) * 128], gyv[:, k * CH:(k + 1) * CH], start=(k == 0), stop=(k == NK - 1))
                    ins.then_inc(pt, 1)
                block.tensor(_); C["PT"] += 1
                finish_tile("wb", slw, buf, pendw, C["PT"])
                def _(v, m=m, b=b, w=C["PT"]):
                    v.wait_ge(pt, w)
                    sl = slice(m * CH, (m + 1) * CH)
                    v.tensor_add(acc[:, sl], acc[:, sl], bank(b)).then_inc(pv, 1)
                block.vector(_); C["PV"] += 1
                bank_guard[b] = ("pv", C["PV"])
            pv_acc_all = C["PV"]
            buf_guard["gyv"] = ("pt", C["PT"])

            last_cast = 0
            for tt in range(4):
                def _(s, tt=tt, w=pv_acc_all, g=buf_guard.get("cast_buf")):
                    if tt == 0:
                        s.wait_ge(pv, w)
                    buf_wait(s, g)
                    s.copy(cast_buf[:].rearrange("p (m q) -> p m q", m=16),
                           acc[:].rearrange("p (m t) -> p m t", m=16)[:, :, tt * 128:(tt + 1) * 128]).then_inc(ps, 1)
                block.scalar(_); C["PS"] += 1
                w_cast = C["PS"]
                last_cast = w_cast
                def _(t, w=w_cast):
                    t.wait_ge(ps, w)
                    pe_wait(t, [6, 7])
                    for m in range(NK):
                        ins = t.transpose(pair_bf[:, m * 128:(m + 1) * 128], cast_buf[:, m * 128:(m + 1) * 128], ident_b[:])
                    ins.then_inc(pt, 1)
                block.tensor(_); C["PT"] += 1
                buf_guard["cast_buf"] = ("pt", C["PT"])
                w_tr = C["PT"]
                x_tm = x_tms[tt % 2]
                def _(sy, tt=tt, row0=row0, x_tm=x_tm, g=buf_guard.get(("x_tm", tt % 2))):
                    buf_wait(sy, g)
                    sy.dma_start(out=x_tm[:], in_=x_in[row0 + tt * 128: row0 + (tt + 1) * 128, :]).then_inc(sd, 16)
                block.sync(_); C["SD"] += 16
                sd_x2 = C["SD"]
                def _(v, x_tm=x_tm, w_tr=w_tr, sd_x2=sd_x2):
                    v.wait_ge(pt, w_tr)
                    v.wait_ge(sd, sd_x2)
                    v.tensor_add(x_tm[:], x_tm[:], pair_bf).then_inc(pv, 1)
                block.vector(_); C["PV"] += 1
                bank_guard[6] = ("pv", C["PV"])
                bank_guard[7] = ("pv", C["PV"])
                def _(sy, tt=tt, row0=row0, x_tm=x_tm, w=C["PV"]):
                    sy.wait_ge(pv, w)
                    sy.dma_start(out=out_t[row0 + tt * 128: row0 + (tt + 1) * 128, :], in_=x_tm[:]).then_inc(sd, 16)
                block.sync(_); C["SD"] += 16
                buf_guard[("x_tm", tt % 2)] = ("sd", C["SD"])

            if c == 0:
                chunk0[0] = False
                wb_total = C["WD"]
                def _(sy, w=wb_total):
                    sy.wait_ge(wd, w)
                block.sync(_)
                def _(g, w=wb_total):
                    g.wait_ge(wd, w)
                block.gpsimd(_)

            buf_guard["xnT"] = ("pt", C["PT"])
            buf_guard["acc"] = ("ps", last_cast)
            buf_guard["comb"] = ("pv", g_done[8] if g_done[8] else C["PV"])
            buf_guard["xc16"] = ("pv", tail_scan)

        def _(sy, w=C["SD"]):
            sy.wait_ge(sd, w)
        block.sync(_)

    return nc


_NC_CACHE = None


def _get_nc():
    global _NC_CACHE
    if _NC_CACHE is None:
        _NC_CACHE = build_nc()
    return _NC_CACHE


def kernel(**inputs):
    x = np.ascontiguousarray(np.asarray(inputs["L0_attn_x"], dtype=np.float32))
    W_xy = np.asarray(inputs["L0_attn_W_xy"], np.float32)
    ig_w = np.ascontiguousarray(np.asarray(inputs["L0_attn_ig_w"], np.float32))
    ig_b = np.asarray(inputs["L0_attn_ig_b"], np.float32)
    ag_w = np.ascontiguousarray(np.asarray(inputs["L0_attn_ag_w"], np.float32))
    ag_b = np.asarray(inputs["L0_attn_ag_b"], np.float32)
    a_param = np.asarray(inputs["L0_attn_a_param"], np.float32)
    conv_w = np.asarray(inputs["L0_attn_conv_w"], np.float32)
    conv_b = np.asarray(inputs["L0_attn_conv_b"], np.float32)
    W_resid = np.asarray(inputs["L0_attn_W_resid"], np.float32)
    gate_w = np.asarray(inputs["L0_ffn_gate_w"], np.float32)
    w1 = np.asarray(inputs["L0_ffn_w1"], np.float32)
    w2 = np.asarray(inputs["L0_ffn_w2"], np.float32)
    w3 = np.asarray(inputs["L0_ffn_w3"], np.float32)
    ws1 = np.asarray(inputs["L0_ffn_ws1"], np.float32)
    ws2 = np.asarray(inputs["L0_ffn_ws2"], np.float32)
    ws3 = np.asarray(inputs["L0_ffn_ws3"], np.float32)

    wxy_b = np.ascontiguousarray(W_xy.reshape(32, 128, 16, 128).transpose(0, 3, 2, 1).reshape(32, 128, 2048))
    wr_b = np.ascontiguousarray(W_resid.reshape(16, 128, 16, 128).transpose(0, 3, 2, 1).reshape(16, 128, 2048))
    w1s = np.concatenate([w1, ws1[None]], 0)
    w3s = np.concatenate([w3, ws3[None]], 0)
    w2s = np.concatenate([w2, ws2[None]], 0)
    w1_a = w1s.reshape(9, 8, 128, 16, 128).transpose(0, 1, 4, 3, 2).reshape(9, 8, 128, 2048)
    w3_a = w3s.reshape(9, 8, 128, 16, 128).transpose(0, 1, 4, 3, 2).reshape(9, 8, 128, 2048)
    w13_a = np.ascontiguousarray(np.stack([w1_a, w3_a], axis=2))
    w2_a = np.ascontiguousarray(w2s.reshape(9, 16, 128, 8, 128).transpose(0, 1, 4, 3, 2).reshape(9, 16, 128, 1024))
    gw_t = np.ascontiguousarray(gate_w.T.reshape(16, 128, 8))
    igb_t = np.ascontiguousarray(ig_b.T)
    agb_t = np.ascontiguousarray(ag_b.T)
    cw_t = np.ascontiguousarray(conv_w.T.reshape(16, 128, 4))
    cb_t = np.ascontiguousarray(conv_b.reshape(16, 128).T)
    ap_t = np.ascontiguousarray(a_param.reshape(16, 128).T)
    ident_f32 = np.eye(128, dtype=np.float32)
    ident_bf = np.eye(128, dtype=np.float32).astype(BF)
    onehot = np.zeros((8, 1024), dtype=BF)
    for e in range(8):
        onehot[e, e * 128:(e + 1) * 128] = 1.0

    shared = dict(
        wxy_blk=wxy_b, wr_blk=wr_b, w13_all=w13_a, w2_all=w2_a,
        gw_t=gw_t, igw=ig_w, agw=ag_w, igb_t=igb_t, agb_t=agb_t,
        cw_t=cw_t, cb_t=cb_t, ap_t=ap_t,
        ident_f32=ident_f32, ident_bf=ident_bf, onehot=onehot,
    )
    in_maps = []
    for core in range(8):
        m = dict(shared)
        m["x"] = np.ascontiguousarray(x[2 * core: 2 * core + 2].reshape(TCORE, H))
        in_maps.append(m)

    nc = _get_nc()
    res = run_bass_kernel_spmd(nc, in_maps, core_ids=list(range(8)))
    outs = [np.asarray(res.results[i]["out"]).reshape(2, 2048, H) for i in range(8)]
    return np.concatenate(outs, axis=0)


# version 10
# speedup vs baseline: 1.1390x; 1.1390x over previous
from contextlib import ExitStack

import numpy as np
import ml_dtypes
import concourse.bass as bass
import concourse.mybir as mybir
from concourse.bass_utils import run_bass_kernel_spmd

F32 = mybir.dt.float32
BF16 = mybir.dt.bfloat16
AF = mybir.ActivationFunctionType
ALU = mybir.AluOpType
BF = ml_dtypes.bfloat16
AX = mybir.AxisListType

H = 2048
NK = 16
CH = 512
NCH = 8
TCORE = 4096


def build_nc():
    nc = bass.Bass("TRN2", target_bir_lowering=False)

    x_in = nc.dram_tensor("x", [TCORE, H], F32, kind="ExternalInput").ap()
    wxy_blk = nc.dram_tensor("wxy_blk", [32, 128, 2048], F32, kind="ExternalInput").ap()
    wr_blk = nc.dram_tensor("wr_blk", [16, 128, 2048], F32, kind="ExternalInput").ap()
    w13_all = nc.dram_tensor("w13_all", [9, 8, 2, 128, 2048], F32, kind="ExternalInput").ap()
    w2_all = nc.dram_tensor("w2_all", [9, 16, 128, 1024], F32, kind="ExternalInput").ap()
    gw_t = nc.dram_tensor("gw_t", [NK, 128, 8], F32, kind="ExternalInput").ap()
    igw_in = nc.dram_tensor("igw", [16, 128, 128], F32, kind="ExternalInput").ap()
    agw_in = nc.dram_tensor("agw", [16, 128, 128], F32, kind="ExternalInput").ap()
    igb_in = nc.dram_tensor("igb_t", [128, 16], F32, kind="ExternalInput").ap()
    agb_in = nc.dram_tensor("agb_t", [128, 16], F32, kind="ExternalInput").ap()
    cw_in = nc.dram_tensor("cw_t", [16, 128, 4], F32, kind="ExternalInput").ap()
    cb_in = nc.dram_tensor("cb_t", [128, 16], F32, kind="ExternalInput").ap()
    ap_in = nc.dram_tensor("ap_t", [128, 16], F32, kind="ExternalInput").ap()
    idf_in = nc.dram_tensor("ident_f32", [128, 128], F32, kind="ExternalInput").ap()
    idb_in = nc.dram_tensor("ident_bf", [128, 128], BF16, kind="ExternalInput").ap()
    oh_in = nc.dram_tensor("onehot", [8, 1024], BF16, kind="ExternalInput").ap()

    out_t = nc.dram_tensor("out", [TCORE, H], F32, kind="ExternalOutput").ap()
    wxy_bf = nc.dram_tensor("wxy_bf", [32, 128, 2048], BF16, kind="Internal").ap()
    wr_bf = nc.dram_tensor("wr_bf", [16, 128, 2048], BF16, kind="Internal").ap()
    w13_bf = nc.dram_tensor("w13_bf", [9, 8, 2, 128, 2048], BF16, kind="Internal").ap()
    w2_bf = nc.dram_tensor("w2_bf", [9, 16, 128, 1024], BF16, kind="Internal").ap()

    C = {"SD": 0, "GD": 0, "PT": 0, "PV": 0, "PS": 0, "WD": 0}
    bank_guard = {}
    buf_guard = {}

    with ExitStack() as _st:
        E = _st.enter_context
        ident_f = E(nc.sbuf_tensor("ident_f", [128, 128], F32))
        ident_b = E(nc.sbuf_tensor("ident_b", [128, 128], BF16))
        onehot_sb = E(nc.sbuf_tensor("onehot_sb", [8, 1024], BF16))
        igw_sb = E(nc.sbuf_tensor("igw_sb", [128, 2048], BF16))
        agw_sb = E(nc.sbuf_tensor("agw_sb", [128, 2048], BF16))
        igb_sb = E(nc.sbuf_tensor("igb_sb", [128, 16], F32))
        agb_sb = E(nc.sbuf_tensor("agb_sb", [128, 16], F32))
        cw_sb = E(nc.sbuf_tensor("cw_sb", [128, 64], F32))
        cb_sb = E(nc.sbuf_tensor("cb_sb", [128, 16], F32))
        ap_sb = E(nc.sbuf_tensor("ap_sb", [128, 16], F32))
        c8_sb = E(nc.sbuf_tensor("c8_sb", [128, 16], F32))
        sc16a = E(nc.sbuf_tensor("sc16a", [128, 16], F32))
        sc16b = E(nc.sbuf_tensor("sc16b", [128, 16], F32))
        eps_col = E(nc.sbuf_tensor("eps_col", [128, 1], F32))
        gwT_sb = E(nc.sbuf_tensor("gwT_sb", [128, 128], BF16))

        xnT = E(nc.sbuf_tensor("xnT", [128, NK * CH], BF16))
        gyv = E(nc.sbuf_tensor("gyv", [128, NK * CH], BF16))
        acc = E(nc.sbuf_tensor("acc", [128, NK * CH], F32))
        xc16 = E(nc.sbuf_tensor("xc16", [128, NK * CH], F32))
        xcb_rot = E(nc.sbuf_tensor("xcb_rot", [128, 4 * CH], BF16))
        g_sb = E(nc.sbuf_tensor("g_sb", [128, 8 * CH], BF16))
        comb_sb = E(nc.sbuf_tensor("comb_sb", [128, 8 * CH], BF16))
        ga_g = E(nc.sbuf_tensor("ga_g", [128, 4 * CH], F32))
        u_g = E(nc.sbuf_tensor("u_g", [128, 4 * CH], F32))
        gx_g = E(nc.sbuf_tensor("gx_g", [128, 4 * CH], BF16))
        conv_carry = E(nc.sbuf_tensor("conv_carry", [128, 48], F32))
        scan_carry = E(nc.sbuf_tensor("scan_carry", [128, 16], F32))
        x_tm0 = E(nc.sbuf_tensor("x_tm0", [128, H], F32))
        x_tm1 = E(nc.sbuf_tensor("x_tm1", [128, H], F32))
        xm_t = E(nc.sbuf_tensor("xm_t", [128, H], F32))
        xnbf = E(nc.sbuf_tensor("xnbf", [128, H], BF16))
        mcol = E(nc.sbuf_tensor("mcol", [128, 1], F32))
        vcol = E(nc.sbuf_tensor("vcol", [128, 1], F32))
        wb0 = E(nc.sbuf_tensor("wb0", [128, 2048], BF16))
        wb1 = E(nc.sbuf_tensor("wb1", [128, 2048], BF16))
        wb2 = E(nc.sbuf_tensor("wb2", [128, 2048], BF16))
        wb3 = E(nc.sbuf_tensor("wb3", [128, 2048], BF16))
        v2b0 = E(nc.sbuf_tensor("v2b0", [128, 1024], BF16))
        v2b1 = E(nc.sbuf_tensor("v2b1", [128, 1024], BF16))
        v2b2 = E(nc.sbuf_tensor("v2b2", [128, 1024], BF16))
        v2b3 = E(nc.sbuf_tensor("v2b3", [128, 1024], BF16))
        silu_b = E(nc.sbuf_tensor("silu_b", [128, CH], F32))
        scores = E(nc.sbuf_tensor("scores", [128, 32], F32))
        escr = E(nc.sbuf_tensor("escr", [128, 32], F32))
        p_sb = E(nc.sbuf_tensor("p_sb", [128, 32], F32))
        pm_sb = E(nc.sbuf_tensor("pm_sb", [128, 32], F32))
        combtm = E(nc.sbuf_tensor("combtm", [128, 32], F32))
        stat4a = E(nc.sbuf_tensor("stat4a", [128, 4], F32))
        stat4b = E(nc.sbuf_tensor("stat4b", [128, 4], F32))
        stat4c = E(nc.sbuf_tensor("stat4c", [128, 4], F32))
        scr8 = E(nc.sbuf_tensor("scr8", [128, 8], F32))
        comb_fm = E(nc.sbuf_tensor("comb_fm", [8, 512], BF16))
        cast_buf = E(nc.sbuf_tensor("cast_buf", [128, 2048], BF16))
        psum = E(nc.psum_tensor("psum", [128, 4096], F32))
        sd = E(nc.semaphore("sd"))
        wd = E(nc.semaphore("wd"))
        wbsem = [E(nc.semaphore(f"wbs{i}")) for i in range(4)]
        vbsem = [E(nc.semaphore(f"vbs{i}")) for i in range(4)]
        gd = E(nc.semaphore("gd"))
        pt = E(nc.semaphore("pt"))
        pv = E(nc.semaphore("pv"))
        ps = E(nc.semaphore("ps"))
        block = E(nc.Block())
        psum_bf = psum[:].bitcast(BF16)

        wbs = [wb0, wb1, wb2, wb3]
        v2bs = [v2b0, v2b1, v2b2, v2b3]
        x_tms = [x_tm0, x_tm1]

        def bank(b):
            return psum[:, b * 512:(b + 1) * 512]

        pair_bf = psum_bf[:, 3 * 2048:4 * 2048]

        SEMS = {"pv": pv, "ps": ps, "sd": sd, "pt": pt, "gd": gd, "wd": wd}
        for _i in range(4):
            SEMS[f"wb{_i}"] = wbsem[_i]
            SEMS[f"vb{_i}"] = vbsem[_i]
            C[f"WB{_i}"] = 0
            C[f"VB{_i}"] = 0

        def buf_wait(eng, g):
            if g is not None:
                eng.wait_ge(SEMS[g[0]], g[1])

        def pe_wait(t, banks):
            need = {}
            for b in banks:
                if b in bank_guard:
                    s, v = bank_guard[b]
                    need[s] = max(need.get(s, 0), v)
            for s, v in need.items():
                t.wait_ge(SEMS[s], v)

        def _(sy):
            sy.dma_start(out=ident_f[:], in_=idf_in).then_inc(sd, 16)
            sy.dma_start(out=ident_b[:], in_=idb_in).then_inc(sd, 16)
            sy.dma_start(out=onehot_sb[:], in_=oh_in).then_inc(sd, 16)
            sy.dma_start(out=igb_sb[:], in_=igb_in).then_inc(sd, 16)
            sy.dma_start(out=agb_sb[:], in_=agb_in).then_inc(sd, 16)
            sy.dma_start(out=cw_sb[:].rearrange("p (n i) -> p n i", n=16), in_=cw_in.rearrange("n p i -> p n i")).then_inc(sd, 16)
            sy.dma_start(out=cb_sb[:], in_=cb_in).then_inc(sd, 16)
            sy.dma_start(out=ap_sb[:], in_=ap_in).then_inc(sd, 16)
        block.sync(_); C["SD"] += 16 * 8
        sd_prologue = C["SD"]

        def _(g):
            g.dma_start(out=igw_sb[:].rearrange("p (n j) -> p n j", n=16), in_=igw_in.rearrange("n i j -> i n j")).then_inc(gd, 16)
            g.dma_start(out=agw_sb[:].rearrange("p (n j) -> p n j", n=16), in_=agw_in.rearrange("n i j -> i n j")).then_inc(gd, 16)
            g.dma_start(out=gwT_sb[:].rearrange("p (k e) -> p k e", k=16), in_=gw_t.rearrange("k p e -> p k e")).then_inc(gd, 16)
        block.gpsimd(_); C["GD"] += 16 * 3
        gd_prologue = C["GD"]

        def _(s):
            s.wait_ge(sd, sd_prologue)
            s.activation(sc16a[:], ap_sb[:], AF.Abs)
            s.activation(sc16a[:], sc16a[:], AF.Exp, scale=-1.0)
            s.activation(sc16a[:], sc16a[:], AF.Ln, bias=1.0).then_inc(ps, 1)
        block.scalar(_); C["PS"] += 1
        ps_c8 = C["PS"]
        def _(v):
            v.wait_ge(ps, ps_c8)
            v.memset(eps_col[:], 1e-5)
            v.tensor_scalar_max(sc16b[:], ap_sb[:], 0.0).then_inc(pv, 1)
            v.wait_ge(pv, 1)
            v.tensor_add(c8_sb[:], sc16a[:], sc16b[:]).then_inc(pv, 1)
            v.wait_ge(pv, 2)
            v.tensor_scalar_mul(c8_sb[:], c8_sb[:], -8.0).then_inc(pv, 1)
        block.vector(_); C["PV"] += 3

        big_idx = [0]
        w2_idx = [0]
        chunk0 = [True]

        def _slot_sem(kind, sl):
            return ("wb" if kind == "wb" else "vb") + str(sl)

        def _emit_tile(idx_box, bufs, kind, src_f32, src_bf):
            sl = idx_box[0] % 4
            buf = bufs[sl]
            gk = buf_guard.get((kind, sl))
            ss = _slot_sem(kind, sl)
            ck = ss.upper()
            if chunk0[0]:
                def _(g, buf=buf, src=src_f32, gk=gk):
                    buf_wait(g, gk)
                    g.dma_start(out=buf[:], in_=src).then_inc(gd, 16)
                block.gpsimd(_); C["GD"] += 16
                waits = (("gd", C["GD"]),)
                pend = src_bf
            else:
                if idx_box[0] % 2 == 0:
                    def _(sy, buf=buf, src=src_bf, gk=gk, ss=ss):
                        buf_wait(sy, gk)
                        sy.dma_start(out=buf[:], in_=src).then_inc(SEMS[ss], 16)
                    block.sync(_); C[ck] += 16
                    waits = ((ss, C[ck]),)
                else:
                    def _(g, buf=buf, src=src_bf, gk=gk):
                        buf_wait(g, gk)
                        g.dma_start(out=buf[:], in_=src).then_inc(gd, 16)
                    block.gpsimd(_); C["GD"] += 16
                    waits = (("gd", C["GD"]),)
                pend = None
            idx_box[0] += 1
            return waits, buf, sl, pend

        def emit_big_dma(src_f32, src_bf):
            return _emit_tile(big_idx, wbs, "wb", src_f32, src_bf)

        def emit_w2_dma(src_f32, src_bf):
            return _emit_tile(w2_idx, v2bs, "v2b", src_f32, src_bf)

        def finish_tile(kind, sl, buf, pend, w_pt):
            if pend is not None:
                ss = _slot_sem(kind, sl)
                ck = ss.upper()
                def _(sy, buf=buf, dst=pend, w=w_pt, ss=ss):
                    sy.wait_ge(pt, w)
                    sy.dma_start(out=dst, in_=buf[:]).then_inc(SEMS[ss], 16)
                block.sync(_); C[ck] += 16
                buf_guard[(kind, sl)] = (ss, C[ck])
            else:
                buf_guard[(kind, sl)] = ("pt", w_pt)

        def tile_waits(t, waits):
            for s, v in waits:
                t.wait_ge(SEMS[s], v)

        for c in range(NCH):
            row0 = c * CH
            batch_start = (c % 4 == 0)

            for tt in range(4):
                x_tm = x_tms[tt % 2]
                def _(sy, x_tm=x_tm, tt=tt, row0=row0, g=buf_guard.get(("x_tm", tt % 2))):
                    buf_wait(sy, g)
                    sy.dma_start(out=x_tm[:], in_=x_in[row0 + tt * 128: row0 + (tt + 1) * 128, :]).then_inc(sd, 16)
                block.sync(_); C["SD"] += 16
                sd_x = C["SD"]

                def _(v, x_tm=x_tm, sd_x=sd_x, c0=C["PV"]):
                    v.wait_ge(sd, sd_x)
                    v.reduce_sum(mcol[:], x_tm[:], AX.X).then_inc(pv, 1)
                    v.wait_ge(pv, c0 + 1)
                    v.tensor_scalar_mul(mcol[:], mcol[:], 1.0 / H).then_inc(pv, 1)
                    v.wait_ge(pv, c0 + 2)
                    v.tensor_scalar(xm_t[:], x_tm[:], mcol[:], None, ALU.subtract).then_inc(pv, 1)
                block.vector(_); C["PV"] += 3
                pv_xm = C["PV"]

                def _(s, x_tm=x_tm, pv_xm=pv_xm):
                    s.wait_ge(pv, pv_xm)
                    s.activation(x_tm[:], xm_t[:], AF.Square, accum_out=vcol[:])
                    s.activation(vcol[:], vcol[:], AF.Ln, scale=1.0 / H, bias=eps_col[:])
                    s.activation(vcol[:], vcol[:], AF.Exp, scale=-0.5).then_inc(ps, 1)
                block.scalar(_); C["PS"] += 1
                ps_rstd = C["PS"]
                buf_guard[("x_tm", tt % 2)] = ("ps", ps_rstd)

                def _(v, ps_rstd=ps_rstd, gnb=buf_guard.get("xnbf")):
                    v.wait_ge(ps, ps_rstd)
                    buf_wait(v, gnb)
                    v.tensor_scalar(xnbf[:], xm_t[:], vcol[:], None, ALU.mult).then_inc(pv, 1)
                block.vector(_); C["PV"] += 1
                pv_xn = C["PV"]

                def _(t, pv_xn=pv_xn, tt=tt, gx=buf_guard.get("xnT") if tt == 0 else None):
                    t.wait_ge(pv, pv_xn)
                    if gx is not None:
                        buf_wait(t, gx)
                    pe_wait(t, [6, 7])
                    for k in range(NK):
                        ins = t.transpose(pair_bf[:, k * 128:(k + 1) * 128], xnbf[:, k * 128:(k + 1) * 128], ident_b[:])
                    ins.then_inc(pt, 1)
                block.tensor(_); C["PT"] += 1
                pt_tr = C["PT"]

                def _(v, tt=tt, w=pt_tr):
                    v.wait_ge(pt, w)
                    v.tensor_copy(
                        xnT[:].rearrange("p (k t) -> p k t", k=NK)[:, :, tt * 128:(tt + 1) * 128],
                        pair_bf.rearrange("p (k q) -> p k q", k=NK),
                    ).then_inc(pv, 1)
                block.vector(_); C["PV"] += 1
                bank_guard[6] = ("pv", C["PV"])
                bank_guard[7] = ("pv", C["PV"])
                buf_guard["xnbf"] = ("pt", pt_tr)
            pv_xnT = C["PV"]

            for tloc in range(4):
                b = 4 + tloc % 2
                def _(t, tloc=tloc, b=b, w=pv_xnT, first=(c == 0)):
                    if tloc == 0:
                        t.wait_ge(pv, w)
                        if first:
                            t.wait_ge(gd, gd_prologue)
                    pe_wait(t, [b])
                    for k in range(NK):
                        ins = t.matmul(psum[:, b * 512: b * 512 + 8],
                                       xnT[:, k * CH + tloc * 128: k * CH + (tloc + 1) * 128],
                                       gwT_sb[:, k * 8:(k + 1) * 8],
                                       start=(k == 0), stop=(k == NK - 1))
                    ins.then_inc(pt, 1)
                block.tensor(_); C["PT"] += 1
                def _(v, tloc=tloc, b=b, w=C["PT"]):
                    v.wait_ge(pt, w)
                    v.tensor_copy(scores[:, tloc * 8:(tloc + 1) * 8], psum[:, b * 512: b * 512 + 8]).then_inc(pv, 1)
                block.vector(_); C["PV"] += 1
                bank_guard[b] = ("pv", C["PV"])

            def _(v, w=C["PV"]):
                v.wait_ge(pv, w)
                v.reduce_max(stat4a[:].rearrange("p (t e) -> p t e", e=1),
                             scores[:].rearrange("p (t e) -> p t e", t=4),
                             AX.X, negate=True).then_inc(pv, 1)
            block.vector(_); C["PV"] += 1
            w_negmax = C["PV"]
            def _(s, w=w_negmax):
                s.wait_ge(pv, w)
                for tloc in range(4):
                    ins = s.activation(escr[:, tloc * 8:(tloc + 1) * 8], scores[:, tloc * 8:(tloc + 1) * 8], AF.Exp, bias=stat4a[:, tloc:tloc + 1])
                ins.then_inc(ps, 1)
            block.scalar(_); C["PS"] += 1
            w_exp = C["PS"]
            def _(v, w=w_exp, c0=C["PV"]):
                n = [c0]
                def step(ins):
                    n[0] += 1
                    ins.then_inc(pv, 1)
                    v.wait_ge(pv, n[0])
                v.wait_ge(ps, w)
                step(v.reduce_sum(stat4b[:].rearrange("p (t e) -> p t e", e=1),
                                  escr[:].rearrange("p (t e) -> p t e", t=4), AX.X))
                step(v.reciprocal(stat4c[:], stat4b[:]))
                for tloc in range(4):
                    sl = slice(tloc * 8, (tloc + 1) * 8)
                    ins = v.tensor_scalar(p_sb[:, sl], escr[:, sl], stat4c[:, tloc:tloc + 1], None, ALU.mult)
                step(ins)
                step(v.reduce_max(stat4a[:].rearrange("p (t e) -> p t e", e=1),
                                  p_sb[:].rearrange("p (t e) -> p t e", t=4), AX.X))
                for tloc in range(4):
                    sl = slice(tloc * 8, (tloc + 1) * 8)
                    step(v.tensor_scalar(scr8[:], p_sb[:, sl], stat4a[:, tloc:tloc + 1], None, ALU.is_equal))
                    step(v.scalar_tensor_tensor(pm_sb[:, sl], scr8[:], -1e30, p_sb[:, sl], ALU.mult, ALU.add))
                step(v.reduce_max(stat4b[:].rearrange("p (t e) -> p t e", e=1),
                                  pm_sb[:].rearrange("p (t e) -> p t e", t=4), AX.X))
                for tloc in range(4):
                    sl = slice(tloc * 8, (tloc + 1) * 8)
                    step(v.tensor_scalar(scr8[:], p_sb[:, sl], stat4b[:, tloc:tloc + 1], None, ALU.is_ge))
                    step(v.tensor_mul(combtm[:, sl], scr8[:], p_sb[:, sl]))
            block.vector(_); C["PV"] += 3 + 1 + 8 + 1 + 8
            w_comb = C["PV"]

            def emit_xy_mm(waits, buf, b, rhs):
                def _(t, b=b, buf=buf, waits=waits):
                    tile_waits(t, waits)
                    pe_wait(t, [b])
                    for k in range(NK):
                        ins = t.matmul(bank(b), buf[:, k * 128:(k + 1) * 128], rhs[:, k * CH:(k + 1) * CH], start=(k == 0), stop=(k == NK - 1))
                    ins.then_inc(pt, 1)
                block.tensor(_); C["PT"] += 1
                return C["PT"]

            for idx in range(16):
                r = 16 + idx
                waits, buf, sl, pend = emit_big_dma(wxy_blk[r], wxy_bf[r])
                b = idx % 2
                w_pt = emit_xy_mm(waits, buf, b, xnT)
                finish_tile("wb", sl, buf, pend, w_pt)
                def _(s, r=r, b=b, w=w_pt, gg=buf_guard.get("gyv") if idx == 0 else None):
                    s.wait_ge(pt, w)
                    if gg is not None:
                        buf_wait(s, gg)
                    s.activation(gyv[:, (r - 16) * 512:(r - 16 + 1) * 512], bank(b), AF.Gelu).then_inc(ps, 1)
                block.scalar(_); C["PS"] += 1
                bank_guard[b] = ("ps", C["PS"])

            for tloc in range(4):
                b = 4 + tloc % 2
                def _(t, tloc=tloc, b=b, w=w_comb):
                    if tloc == 0:
                        t.wait_ge(pv, w)
                    pe_wait(t, [b])
                    t.transpose(psum[0:8, b * 512: b * 512 + 128], combtm[:, tloc * 8:(tloc + 1) * 8], ident_f[:]).then_inc(pt, 1)
                block.tensor(_); C["PT"] += 1
                def _(v, tloc=tloc, b=b, w=C["PT"]):
                    v.wait_ge(pt, w)
                    v.tensor_copy(comb_fm[:, tloc * 128:(tloc + 1) * 128], psum[0:8, b * 512: b * 512 + 128]).then_inc(pv, 1)
                block.vector(_); C["PV"] += 1
                bank_guard[b] = ("pv", C["PV"])
            w_combfm = C["PV"]

            for e in range(8):
                b = 4 + e % 2
                def _(t, e=e, b=b, w=w_combfm):
                    if e == 0:
                        t.wait_ge(pv, w)
                    pe_wait(t, [b])
                    t.matmul(bank(b), onehot_sb[:, e * 128:(e + 1) * 128], comb_fm[:], start=True, stop=True).then_inc(pt, 1)
                block.tensor(_); C["PT"] += 1
                def _(v, e=e, b=b, w=C["PT"], gc=buf_guard.get("comb") if e == 0 else None):
                    v.wait_ge(pt, w)
                    if gc is not None:
                        buf_wait(v, gc)
                    v.tensor_copy(comb_sb[:, e * CH:(e + 1) * CH], bank(b)).then_inc(pv, 1)
                block.vector(_); C["PV"] += 1
                bank_guard[b] = ("pv", C["PV"])

            if batch_start:
                def _(v):
                    v.memset(conv_carry[:], 0.0)
                    v.memset(scan_carry[:], 0.0)
                block.vector(_)

            conv_done = [0] * 16
            for r in range(16):
                waits, buf, sl, pend = emit_big_dma(wxy_blk[r], wxy_bf[r])
                b = r % 2
                w_pt = emit_xy_mm(waits, buf, b, xnT)
                finish_tile("wb", sl, buf, pend, w_pt)
                def _(v, r=r, b=b, w=w_pt, gx=buf_guard.get("xc16") if r == 0 else None):
                    v.wait_ge(pt, w)
                    if gx is not None:
                        buf_wait(v, gx)
                    sl2 = slice(r * 512, (r + 1) * 512)
                    pb = bank(b)
                    v.tensor_scalar(xc16[:, sl2], pb, cw_sb[:, r * 4 + 3:r * 4 + 4], cb_sb[:, r:r + 1], ALU.mult, ALU.add)
                    v.scalar_tensor_tensor(xc16[:, r * 512 + 1:(r + 1) * 512], pb[:, 0:511], cw_sb[:, r * 4 + 2:r * 4 + 3], xc16[:, r * 512 + 1:(r + 1) * 512], ALU.mult, ALU.add)
                    v.scalar_tensor_tensor(xc16[:, r * 512 + 2:(r + 1) * 512], pb[:, 0:510], cw_sb[:, r * 4 + 1:r * 4 + 2], xc16[:, r * 512 + 2:(r + 1) * 512], ALU.mult, ALU.add)
                    v.scalar_tensor_tensor(xc16[:, r * 512 + 3:(r + 1) * 512], pb[:, 0:509], cw_sb[:, r * 4 + 0:r * 4 + 1], xc16[:, r * 512 + 3:(r + 1) * 512], ALU.mult, ALU.add)
                    cc = conv_carry[:, r * 3:(r + 1) * 3]
                    v.scalar_tensor_tensor(xc16[:, r * 512:r * 512 + 1], cc[:, 2:3], cw_sb[:, r * 4 + 2:r * 4 + 3], xc16[:, r * 512:r * 512 + 1], ALU.mult, ALU.add)
                    v.scalar_tensor_tensor(xc16[:, r * 512:r * 512 + 2], cc[:, 1:3], cw_sb[:, r * 4 + 1:r * 4 + 2], xc16[:, r * 512:r * 512 + 2], ALU.mult, ALU.add)
                    v.scalar_tensor_tensor(xc16[:, r * 512:r * 512 + 3], cc[:, 0:3], cw_sb[:, r * 4 + 0:r * 4 + 1], xc16[:, r * 512:r * 512 + 3], ALU.mult, ALU.add)
                    v.tensor_copy(cc, pb[:, 509:512]).then_inc(pv, 1)
                block.vector(_); C["PV"] += 1
                conv_done[r] = C["PV"]
                bank_guard[b] = ("pv", C["PV"])

            sig_done = [0] * 16
            la_done = [0] * 16
            asq_done = [0] * 16
            om_done = [0] * 16
            mult_done = [0] * 16
            scan_done = [0] * 16
            g_done = [0] * 9

            def emit_w13(e):
                done = 0
                for m in range(8):
                    wt1, buf1, sl1, pend1 = emit_big_dma(w13_all[e, m, 0], w13_bf[e, m, 0])
                    wt3, buf3, sl3, pend3 = emit_big_dma(w13_all[e, m, 1], w13_bf[e, m, 1])
                    def _(t, buf1=buf1, buf3=buf3, wt1=wt1, wt3=wt3):
                        tile_waits(t, wt1)
                        tile_waits(t, wt3)
                        pe_wait(t, [2, 3])
                        for k in range(NK):
                            ins = t.matmul(bank(2), buf1[:, k * 128:(k + 1) * 128], xnT[:, k * CH:(k + 1) * CH], start=(k == 0), stop=(k == NK - 1))
                        ins.then_inc(pt, 1)
                        for k in range(NK):
                            ins = t.matmul(bank(3), buf3[:, k * 128:(k + 1) * 128], xnT[:, k * CH:(k + 1) * CH], start=(k == 0), stop=(k == NK - 1))
                        ins.then_inc(pt, 1)
                    block.tensor(_); C["PT"] += 2
                    finish_tile("wb", sl1, buf1, pend1, C["PT"])
                    finish_tile("wb", sl3, buf3, pend3, C["PT"])
                    def _(s, w=C["PT"], gsl=buf_guard.get("silu_b")):
                        s.wait_ge(pt, w)
                        buf_wait(s, gsl)
                        s.activation(silu_b[:], bank(2), AF.Silu).then_inc(ps, 1)
                    block.scalar(_); C["PS"] += 1
                    bank_guard[2] = ("ps", C["PS"])
                    def _(v, e=e, m=m, w=C["PS"]):
                        v.wait_ge(ps, w)
                        if e < 8:
                            v.tensor_mul(silu_b[:], silu_b[:], bank(3))
                            v.tensor_mul(g_sb[:, m * CH:(m + 1) * CH], silu_b[:], comb_sb[:, e * CH:(e + 1) * CH]).then_inc(pv, 1)
                        else:
                            v.tensor_mul(g_sb[:, m * CH:(m + 1) * CH], silu_b[:], bank(3)).then_inc(pv, 1)
                    block.vector(_); C["PV"] += 1
                    bank_guard[3] = ("pv", C["PV"])
                    buf_guard["silu_b"] = ("pv", C["PV"])
                    done = C["PV"]
                return done

            def emit_w2(e, first):
                last = 0
                for m2 in range(16):
                    wt2, buf2, sl2, pend2 = emit_w2_dma(w2_all[e, m2], w2_bf[e, m2])
                    b = 4 + m2 % 2
                    def _(t, buf2=buf2, wt2=wt2, b=b, m2=m2, wg=g_done[e]):
                        tile_waits(t, wt2)
                        if m2 == 0:
                            t.wait_ge(pv, wg)
                        pe_wait(t, [b])
                        for k in range(8):
                            ins = t.matmul(bank(b), buf2[:, k * 128:(k + 1) * 128], g_sb[:, k * CH:(k + 1) * CH], start=(k == 0), stop=(k == 7))
                        ins.then_inc(pt, 1)
                    block.tensor(_); C["PT"] += 1
                    finish_tile("v2b", sl2, buf2, pend2, C["PT"])
                    def _(v, m2=m2, b=b, w=C["PT"], first=first, ga=buf_guard.get("acc") if (first and m2 == 0) else None):
                        v.wait_ge(pt, w)
                        if ga is not None:
                            buf_wait(v, ga)
                        sl = slice(m2 * CH, (m2 + 1) * CH)
                        if first:
                            v.tensor_copy(acc[:, sl], bank(b)).then_inc(pv, 1)
                        else:
                            v.tensor_add(acc[:, sl], acc[:, sl], bank(b)).then_inc(pv, 1)
                    block.vector(_); C["PV"] += 1
                    bank_guard[b] = ("pv", C["PV"])
                    last = C["PV"]
                return last

            def emit_wave(g4):
                cast_done = [0] * 4
                for j4 in range(4):
                    r = 4 * g4 + j4
                    def _(s, r=r, w=conv_done[r], gk=buf_guard.get(("xcb", r % 4))):
                        s.wait_ge(pv, w)
                        buf_wait(s, gk)
                        s.copy(xcb_rot[:, (r % 4) * 512:(r % 4 + 1) * 512], xc16[:, r * 512:(r + 1) * 512]).then_inc(ps, 1)
                    block.scalar(_); C["PS"] += 1
                    cast_done[j4] = C["PS"]
                for j4 in range(4):
                    r = 4 * g4 + j4
                    def _(t, r=r, w=cast_done[j4]):
                        t.wait_ge(ps, w)
                        pe_wait(t, [6, 7])
                        xs = xcb_rot[:, (r % 4) * 512:(r % 4 + 1) * 512]
                        t.matmul(bank(6), igw_sb[:, r * 128:(r + 1) * 128], xs, start=True, stop=True).then_inc(pt, 1)
                        t.matmul(bank(7), agw_sb[:, r * 128:(r + 1) * 128], xs, start=True, stop=True).then_inc(pt, 1)
                    block.tensor(_); C["PT"] += 2
                    buf_guard[("xcb", r % 4)] = ("pt", C["PT"])
                    def _(s, r=r, w=C["PT"], gsl=buf_guard.get(("gslot", r % 4))):
                        s.wait_ge(pt, w)
                        buf_wait(s, gsl)
                        j = r % 4
                        s.activation(gx_g[:, j * 512:(j + 1) * 512], bank(6), AF.Sigmoid, bias=igb_sb[:, r:r + 1])
                        s.activation(ga_g[:, j * 512:(j + 1) * 512], bank(7), AF.Sigmoid, bias=agb_sb[:, r:r + 1]).then_inc(ps, 1)
                    block.scalar(_); C["PS"] += 1
                    sig_done[r] = C["PS"]
                    bank_guard[6] = ("ps", C["PS"])
                    bank_guard[7] = ("ps", C["PS"])
                for j4 in range(4):
                    r = 4 * g4 + j4
                    def _(v, r=r, j=r % 4, w=sig_done[r]):
                        v.wait_ge(ps, w)
                        sl = slice(j * 512, (j + 1) * 512)
                        v.tensor_scalar(ga_g[:, sl], ga_g[:, sl], c8_sb[:, r:r + 1], None, ALU.mult).then_inc(pv, 1)
                    block.vector(_); C["PV"] += 1
                    la_done[r] = C["PV"]
                for j4 in range(4):
                    r = 4 * g4 + j4
                    def _(s, r=r, j=r % 4, w=la_done[r]):
                        s.wait_ge(pv, w)
                        sl = slice(j * 512, (j + 1) * 512)
                        s.activation(ga_g[:, sl], ga_g[:, sl], AF.Exp)
                        s.activation(u_g[:, sl], ga_g[:, sl], AF.Square).then_inc(ps, 1)
                    block.scalar(_); C["PS"] += 1
                    asq_done[r] = C["PS"]
                for j4 in range(4):
                    r = 4 * g4 + j4
                    def _(v, r=r, j=r % 4, w=asq_done[r]):
                        v.wait_ge(ps, w)
                        sl = slice(j * 512, (j + 1) * 512)
                        v.tensor_scalar(u_g[:, sl], u_g[:, sl], -1.0, 1.0, ALU.mult, ALU.add).then_inc(pv, 1)
                    block.vector(_); C["PV"] += 1
                    om_done[r] = C["PV"]
                for j4 in range(4):
                    r = 4 * g4 + j4
                    def _(s, r=r, j=r % 4, w=om_done[r]):
                        s.wait_ge(pv, w)
                        sl = slice(j * 512, (j + 1) * 512)
                        s.activation(u_g[:, sl], u_g[:, sl], AF.Ln)
                        s.activation(u_g[:, sl], u_g[:, sl], AF.Exp, scale=0.5).then_inc(ps, 1)
                    block.scalar(_); C["PS"] += 1
                    mult_done[r] = C["PS"]
                for j4 in range(4):
                    r = 4 * g4 + j4
                    def _(v, r=r, j=r % 4, w=mult_done[r], c0=C["PV"]):
                        v.wait_ge(ps, w)
                        sl = slice(j * 512, (j + 1) * 512)
                        xsl = slice(r * 512, (r + 1) * 512)
                        v.tensor_mul(xc16[:, xsl], xc16[:, xsl], gx_g[:, sl])
                        v.tensor_mul(xc16[:, xsl], xc16[:, xsl], u_g[:, sl])
                        v.tensor_tensor_scan(u_g[:, sl], ga_g[:, sl], xc16[:, xsl], scan_carry[:, r:r + 1], ALU.mult, ALU.add).then_inc(pv, 1)
                        v.wait_ge(pv, c0 + 1)
                        v.tensor_copy(scan_carry[:, r:r + 1], u_g[:, j * 512 + 511:j * 512 + 512])
                        v.tensor_mul(gyv[:, xsl], gyv[:, xsl], u_g[:, sl]).then_inc(pv, 1)
                    block.vector(_); C["PV"] += 2
                    scan_done[r] = C["PV"]
                    buf_guard[("gslot", r % 4)] = ("pv", C["PV"])

            g_done[0] = emit_w13(0)
            emit_wave(0)
            emit_w2(0, first=True)
            for e in range(1, 4):
                g_done[e] = emit_w13(e)
                emit_wave(e)
                emit_w2(e, first=False)
            for e in range(4, 9):
                g_done[e] = emit_w13(e)
                emit_w2(e, first=False)
            tail_scan = scan_done[15]

            for m in range(16):
                waits, buf, slw, pendw = emit_big_dma(wr_blk[m], wr_bf[m])
                b = m % 2
                def _(t, b=b, buf=buf, waits=waits, m=m, ws=tail_scan):
                    tile_waits(t, waits)
                    if m == 0:
                        t.wait_ge(pv, ws)
                    pe_wait(t, [b])
                    for k in range(NK):
                        ins = t.matmul(bank(b), buf[:, k * 128:(k + 1) * 128], gyv[:, k * CH:(k + 1) * CH], start=(k == 0), stop=(k == NK - 1))
                    ins.then_inc(pt, 1)
                block.tensor(_); C["PT"] += 1
                finish_tile("wb", slw, buf, pendw, C["PT"])
                def _(v, m=m, b=b, w=C["PT"]):
                    v.wait_ge(pt, w)
                    sl = slice(m * CH, (m + 1) * CH)
                    v.tensor_add(acc[:, sl], acc[:, sl], bank(b)).then_inc(pv, 1)
                block.vector(_); C["PV"] += 1
                bank_guard[b] = ("pv", C["PV"])
            pv_acc_all = C["PV"]
            buf_guard["gyv"] = ("pt", C["PT"])

            last_cast = 0
            for tt in range(4):
                def _(s, tt=tt, w=pv_acc_all, g=buf_guard.get("cast_buf")):
                    if tt == 0:
                        s.wait_ge(pv, w)
                    buf_wait(s, g)
                    s.copy(cast_buf[:].rearrange("p (m q) -> p m q", m=16),
                           acc[:].rearrange("p (m t) -> p m t", m=16)[:, :, tt * 128:(tt + 1) * 128]).then_inc(ps, 1)
                block.scalar(_); C["PS"] += 1
                w_cast = C["PS"]
                last_cast = w_cast
                def _(t, w=w_cast):
                    t.wait_ge(ps, w)
                    pe_wait(t, [6, 7])
                    for m in range(NK):
                        ins = t.transpose(pair_bf[:, m * 128:(m + 1) * 128], cast_buf[:, m * 128:(m + 1) * 128], ident_b[:])
                    ins.then_inc(pt, 1)
                block.tensor(_); C["PT"] += 1
                buf_guard["cast_buf"] = ("pt", C["PT"])
                w_tr = C["PT"]
                x_tm = x_tms[tt % 2]
                def _(sy, tt=tt, row0=row0, x_tm=x_tm, g=buf_guard.get(("x_tm", tt % 2))):
                    buf_wait(sy, g)
                    sy.dma_start(out=x_tm[:], in_=x_in[row0 + tt * 128: row0 + (tt + 1) * 128, :]).then_inc(sd, 16)
                block.sync(_); C["SD"] += 16
                sd_x2 = C["SD"]
                def _(v, x_tm=x_tm, w_tr=w_tr, sd_x2=sd_x2):
                    v.wait_ge(pt, w_tr)
                    v.wait_ge(sd, sd_x2)
                    v.tensor_add(x_tm[:], x_tm[:], pair_bf).then_inc(pv, 1)
                block.vector(_); C["PV"] += 1
                bank_guard[6] = ("pv", C["PV"])
                bank_guard[7] = ("pv", C["PV"])
                def _(sy, tt=tt, row0=row0, x_tm=x_tm, w=C["PV"]):
                    sy.wait_ge(pv, w)
                    sy.dma_start(out=out_t[row0 + tt * 128: row0 + (tt + 1) * 128, :], in_=x_tm[:]).then_inc(sd, 16)
                block.sync(_); C["SD"] += 16
                buf_guard[("x_tm", tt % 2)] = ("sd", C["SD"])

            if c == 0:
                chunk0[0] = False
                wb_tot = [(s, C[s.upper()]) for s in
                          [f"wb{i}" for i in range(4)] + [f"vb{i}" for i in range(4)]]
                def _(sy, wt=tuple(wb_tot)):
                    for s, v in wt:
                        sy.wait_ge(SEMS[s], v)
                block.sync(_)
                def _(g, wt=tuple(wb_tot)):
                    for s, v in wt:
                        g.wait_ge(SEMS[s], v)
                block.gpsimd(_)

            buf_guard["xnT"] = ("pt", C["PT"])
            buf_guard["acc"] = ("ps", last_cast)
            buf_guard["comb"] = ("pv", g_done[8] if g_done[8] else C["PV"])
            buf_guard["xc16"] = ("pv", tail_scan)

        def _(sy, w=C["SD"]):
            sy.wait_ge(sd, w)
        block.sync(_)

    return nc


_NC_CACHE = None


def _get_nc():
    global _NC_CACHE
    if _NC_CACHE is None:
        _NC_CACHE = build_nc()
    return _NC_CACHE


def kernel(**inputs):
    x = np.ascontiguousarray(np.asarray(inputs["L0_attn_x"], dtype=np.float32))
    W_xy = np.asarray(inputs["L0_attn_W_xy"], np.float32)
    ig_w = np.ascontiguousarray(np.asarray(inputs["L0_attn_ig_w"], np.float32))
    ig_b = np.asarray(inputs["L0_attn_ig_b"], np.float32)
    ag_w = np.ascontiguousarray(np.asarray(inputs["L0_attn_ag_w"], np.float32))
    ag_b = np.asarray(inputs["L0_attn_ag_b"], np.float32)
    a_param = np.asarray(inputs["L0_attn_a_param"], np.float32)
    conv_w = np.asarray(inputs["L0_attn_conv_w"], np.float32)
    conv_b = np.asarray(inputs["L0_attn_conv_b"], np.float32)
    W_resid = np.asarray(inputs["L0_attn_W_resid"], np.float32)
    gate_w = np.asarray(inputs["L0_ffn_gate_w"], np.float32)
    w1 = np.asarray(inputs["L0_ffn_w1"], np.float32)
    w2 = np.asarray(inputs["L0_ffn_w2"], np.float32)
    w3 = np.asarray(inputs["L0_ffn_w3"], np.float32)
    ws1 = np.asarray(inputs["L0_ffn_ws1"], np.float32)
    ws2 = np.asarray(inputs["L0_ffn_ws2"], np.float32)
    ws3 = np.asarray(inputs["L0_ffn_ws3"], np.float32)

    wxy_b = np.ascontiguousarray(W_xy.reshape(32, 128, 16, 128).transpose(0, 3, 2, 1).reshape(32, 128, 2048))
    wr_b = np.ascontiguousarray(W_resid.reshape(16, 128, 16, 128).transpose(0, 3, 2, 1).reshape(16, 128, 2048))
    w1s = np.concatenate([w1, ws1[None]], 0)
    w3s = np.concatenate([w3, ws3[None]], 0)
    w2s = np.concatenate([w2, ws2[None]], 0)
    w1_a = w1s.reshape(9, 8, 128, 16, 128).transpose(0, 1, 4, 3, 2).reshape(9, 8, 128, 2048)
    w3_a = w3s.reshape(9, 8, 128, 16, 128).transpose(0, 1, 4, 3, 2).reshape(9, 8, 128, 2048)
    w13_a = np.ascontiguousarray(np.stack([w1_a, w3_a], axis=2))
    w2_a = np.ascontiguousarray(w2s.reshape(9, 16, 128, 8, 128).transpose(0, 1, 4, 3, 2).reshape(9, 16, 128, 1024))
    gw_t = np.ascontiguousarray(gate_w.T.reshape(16, 128, 8))
    igb_t = np.ascontiguousarray(ig_b.T)
    agb_t = np.ascontiguousarray(ag_b.T)
    cw_t = np.ascontiguousarray(conv_w.T.reshape(16, 128, 4))
    cb_t = np.ascontiguousarray(conv_b.reshape(16, 128).T)
    ap_t = np.ascontiguousarray(a_param.reshape(16, 128).T)
    ident_f32 = np.eye(128, dtype=np.float32)
    ident_bf = np.eye(128, dtype=np.float32).astype(BF)
    onehot = np.zeros((8, 1024), dtype=BF)
    for e in range(8):
        onehot[e, e * 128:(e + 1) * 128] = 1.0

    shared = dict(
        wxy_blk=wxy_b, wr_blk=wr_b, w13_all=w13_a, w2_all=w2_a,
        gw_t=gw_t, igw=ig_w, agw=ag_w, igb_t=igb_t, agb_t=agb_t,
        cw_t=cw_t, cb_t=cb_t, ap_t=ap_t,
        ident_f32=ident_f32, ident_bf=ident_bf, onehot=onehot,
    )
    in_maps = []
    for core in range(8):
        m = dict(shared)
        m["x"] = np.ascontiguousarray(x[2 * core: 2 * core + 2].reshape(TCORE, H))
        in_maps.append(m)

    nc = _get_nc()
    res = run_bass_kernel_spmd(nc, in_maps, core_ids=list(range(8)))
    outs = [np.asarray(res.results[i]["out"]).reshape(2, 2048, H) for i in range(8)]
    return np.concatenate(outs, axis=0)
